# version 03 (final)
# baseline (speedup 1.0000x reference)
import numpy as np
import ml_dtypes
from contextlib import ExitStack

import concourse.bass as bass
import concourse.mybir as mybir
from concourse.bass_utils import run_bass_kernel_spmd

F32 = mybir.dt.float32
I32 = mybir.dt.int32
BF16 = mybir.dt.bfloat16
AF = mybir.ActivationFunctionType
AL = mybir.AluOpType
AX = mybir.AxisListType
bfnp = ml_dtypes.bfloat16

B, T, D = 8, 4096, 1024
NP, dh = 8, 64
TT = 512
NTILE = T // TT
JG = 8
NJG = dh // JG
FH = 8
HID2 = 5632
NAB = 22
HTK = 2816
NRING = 5
PI = float(np.pi)
TWO_PI = float(2.0 * np.pi)

SEMS = (["s_x", "s_xt", "s_kf", "s_pe", "s_ve", "s_sc", "s_gp", "s_gpc", "s_st"]
        + [f"s_w{i}" for i in range(NRING)])
ENG_LIST = ("sync", "vector", "scalar", "tensor", "gpsimd")


class Walk:
    def __init__(self, nc, eng_name, sems):
        self.nc = nc
        self.eng_name = eng_name
        self.eng = {"sync": nc.sync, "vector": nc.vector, "scalar": nc.scalar,
                    "tensor": nc.tensor, "gpsimd": nc.gpsimd}[eng_name]
        self.c = {s: 0 for s in sems}
        self.sem = dict(sems)
        self.last_wait = {}

    def wait(self, engine, sem_name, value=None):
        if value is None:
            value = self.c[sem_name]
        if value <= 0:
            return
        if self.eng_name == engine:
            if self.last_wait.get(sem_name, -1) < value:
                self.eng.wait_ge(self.sem[sem_name], value)
                self.last_wait[sem_name] = value

    def do(self, engine, build, inc=()):
        if self.eng_name == engine:
            ins = build(self.eng)
            for s, n in inc:
                ins = ins.then_inc(self.sem[s], n)
        for s, n in inc:
            self.c[s] += n

    def dma(self, engine, out, in_, sem_name):
        self.do(engine, lambda e: e.dma_start(out=out, in_=in_),
                inc=((sem_name, 16),))

    def barrier(self):
        me = self.eng_name
        pairs = [("s_pe", "tensor"), ("s_ve", "vector"), ("s_sc", "scalar"),
                 ("s_gpc", "gpsimd")]
        for s, owner in pairs:
            if owner != me:
                self.wait(me, s)


def emit(k, dr, g):
    for name, h in g.items():
        globals()
    x_t = g["x_t"]; xn = g["xn"]; xffn = g["xffn"]
    q_at = g["q_at"]; k_at = g["k_at"]; v_at = g["v_at"]
    t1b = g["t1b"]; straj = g["straj"]; fbuf = g["fbuf"]
    scr8b = g["scr8b"]; yatt = g["yatt"]; kqflat = g["kqflat"]
    rt = g["rt"]; gt = g["gt"]; carry = g["carry"]; wsta = g["wsta"]
    ind_c = g["ind_c"]; sumrep_c = g["sumrep_c"]; perm_c = g["perm_c"]
    ones_c = g["ones_c"]; eye_c = g["eye_c"]; onesf_c = g["onesf_c"]
    mask_c = g["mask_c"]; col_c = g["col_c"]
    khist = g["khist"]; vhist = g["vhist"]; htb = g["htb"]; sexpT = g["sexpT"]
    qhf = g["qhf"]; rotT = g["rotT"]; avo = g["avo"]
    sraw = g["sraw"]; sexp1 = g["sexp1"]; vtb = g["vtb"]
    murep = g["murep"]; rsrep = g["rsrep"]; rowbuf = g["rowbuf"]
    bias_qkv = g["bias_qkv"]; bias_out = g["bias_out"]
    bias_b1 = g["bias_b1"]; bias_b2 = g["bias_b2"]
    lngt = g["lngt"]; lnbt = g["lnbt"]; qsc = g["qsc"]; ksc = g["ksc"]
    iot = g["iot"]; tgen = g["tgen"]; tgen2 = g["iot"]
    invf = g["invf"]; lnbase = g["lnbase"]; halo = g["halo"]
    wring = g["wring"]
    pb = g["pb"]; pbt = g["pbt"]; pb5 = g["pb5"]; pbs = g["pbs"]

    def vop(build):
        k.wait("vector", "s_ve")
        k.do("vector", build, inc=(("s_ve", 1),))

    def sop(build):
        k.wait("scalar", "s_sc")
        k.do("scalar", build, inc=(("s_sc", 1),))

    def top(build):
        k.do("tensor", build, inc=(("s_pe", 1),))

    def gop(build):
        k.wait("gpsimd", "s_gpc")
        k.do("gpsimd", build, inc=(("s_gpc", 1),))

    ring_load = [0]
    ring_free_mark = []
    ring_info = []

    def ring_dma(dram_ap):
        slot = ring_load[0] % NRING
        if ring_load[0] >= NRING:
            k.wait("gpsimd", "s_pe", ring_free_mark[ring_load[0] - NRING])
        ncols = dram_ap.shape[-1]
        nrows = dram_ap.shape[-2]
        dst = wring[0:nrows, slot * 1024: slot * 1024 + ncols]
        k.dma("gpsimd", dst, dram_ap, f"s_w{slot}")
        idx = ring_load[0]
        ring_info.append((slot, k.c[f"s_w{slot}"]))
        ring_load[0] += 1
        return slot, idx

    def ring_wait(idx):
        slot, tot = ring_info[idx]
        k.wait("tensor", f"s_w{slot}", tot)

    def ring_done():
        ring_free_mark.append(k.c["s_pe"])

    bank_mark = {}

    def bank_wait(name):
        m = bank_mark.get(name)
        if m is not None:
            k.wait("tensor", m[0], m[1])

    def bank_consumed(name, sem):
        bank_mark[name] = (sem, k.c[sem])

    k.dma("sync", mask_c[:], dr["c_mask"], "s_x")
    k.dma("sync", col_c[:], dr["c_col"], "s_x")
    k.dma("sync", bias_qkv[:], dr["aw_qkv_b"], "s_x")
    k.dma("sync", bias_out[:], dr["aw_out_b"], "s_x")
    k.dma("sync", bias_b1[:], dr["f_b1"], "s_x")
    k.dma("sync", bias_b2[:], dr["f_b2"], "s_x")
    k.dma("sync", lngt[:], dr["f_lng"], "s_x")
    k.dma("sync", lnbt[:], dr["f_lnb"], "s_x")
    k.dma("sync", qsc[:], dr["f_qs"], "s_x")
    k.dma("sync", ksc[:], dr["f_ks"], "s_x")
    k.dma("sync", gt[:], dr["a_gam"], "s_x")
    k.dma("sync", rt[:], dr["a_lam"], "s_x")
    k.dma("sync", onesf_c[0:1, :], dr["c_onesf"], "s_x")
    k.dma("sync", g["eyef_c"][:], dr["c_eye"], "s_x")
    k.dma("gpsimd", ind_c[:], dr["c_ind"], "s_gp")
    k.dma("gpsimd", sumrep_c[:], dr["c_sumrep"], "s_gp")
    k.dma("gpsimd", perm_c[:], dr["c_perm"], "s_gp")
    k.dma("gpsimd", ones_c[:], dr["c_ones1"], "s_gp")
    k.dma("gpsimd", eye_c[:], dr["c_eye"], "s_gp")
    k.dma("gpsimd", wsta[:], dr["a_wsta"], "s_gp")

    k.wait("scalar", "s_x")
    sop(lambda e: e.activation(out=rt[:], in_=rt[:], func=AF.Sigmoid))
    sop(lambda e: e.activation(out=rt[:], in_=rt[:], func=AF.Copy,
                               scale=-1.0, bias=1.0))
    k.wait("vector", "s_x")
    k.wait("vector", "s_gp")
    vop(lambda e: e.tensor_single_scalar(out=qsc[:], in_=qsc[:], scalar=8.0,
                                         op=AL.mult))
    vop(lambda e: e.tensor_single_scalar(out=gt[:], in_=gt[:],
                                         scalar=0.125, op=AL.mult))
    vop(lambda e: e.memset(carry[:], 0.0))
    vop(lambda e: e.memset(khist[:], 0.0))
    vop(lambda e: e.memset(vhist[:], 0.0))
    vop(lambda e: e.memset(xffn[:], 0.0))
    vop(lambda e: e.memset(kqflat[:], 0.0))
    vop(lambda e: e.memset(halo[:], 0.0))
    k.wait("scalar", "s_ve")
    vop(lambda e: e.tensor_single_scalar(
        out=invf[:], in_=col_c[:, 0:1],
        scalar=-2.0 * float(np.log(10000.0)) / 64.0, op=AL.mult))
    vop(lambda e: e.tensor_scalar(out=lnbase[:], in0=col_c[:, 0:1],
                                  scalar1=2.0 / 89.6, scalar2=25.6 / 89.6,
                                  op0=AL.mult, op1=AL.add))
    k.wait("scalar", "s_ve")
    sop(lambda e: e.activation(out=invf[:], in_=invf[:], func=AF.Exp))
    sop(lambda e: e.activation(out=lnbase[:], in_=lnbase[:], func=AF.Ln))
    k.barrier()

    store_mark = []

    def ln_stats(src_bf, sq_dst, eps):
        bank_wait("b5")
        k.wait("tensor", "s_ve")
        k.wait("tensor", "s_sc")
        for c in range(8):
            ap = src_bf(c)
            top(lambda e, ap=ap, c=c: e.matmul(
                pb5[0:1, :], ones_c[:], ap, start=(c == 0), stop=(c == 7)))
        k.wait("vector", "s_pe")
        vop(lambda e: e.tensor_copy(out=murep[0:1, :], in_=pb5[0:1, :]))
        bank_consumed("b5", "s_ve")
        for c in range(8):
            ap = src_bf(c)
            vop(lambda e, ap=ap, c=c: e.tensor_tensor(
                out=sq_dst[:, c * TT:(c + 1) * TT], in0=ap, in1=ap, op=AL.mult))
        bank_wait("b5")
        k.wait("tensor", "s_ve")
        for c in range(8):
            top(lambda e, c=c: e.matmul(
                pb5[0:1, :], ones_c[:], sq_dst[:, c * TT:(c + 1) * TT],
                start=(c == 0), stop=(c == 7)))
        k.wait("vector", "s_pe")
        vop(lambda e: e.tensor_copy(out=rsrep[0:1, :], in_=pb5[0:1, :]))
        bank_consumed("b5", "s_ve")
        vop(lambda e: e.tensor_single_scalar(out=murep[0:1, :], in_=murep[0:1, :],
                                             scalar=1.0 / D, op=AL.mult))
        vop(lambda e: e.tensor_single_scalar(out=rsrep[0:1, :], in_=rsrep[0:1, :],
                                             scalar=1.0 / D, op=AL.mult))
        vop(lambda e: e.tensor_tensor(out=tgen[0:1, 0:TT], in0=murep[0:1, :],
                                      in1=murep[0:1, :], op=AL.mult))
        vop(lambda e: e.tensor_tensor(out=rsrep[0:1, :], in0=rsrep[0:1, :],
                                      in1=tgen[0:1, 0:TT], op=AL.subtract))
        vop(lambda e: e.tensor_single_scalar(out=rsrep[0:1, :],
                                             in_=rsrep[0:1, :], scalar=eps,
                                             op=AL.add))
        k.wait("scalar", "s_ve")
        sop(lambda e: e.activation(out=rsrep[0:1, :], in_=rsrep[0:1, :],
                                   func=AF.Ln))
        k.wait("vector", "s_sc")
        vop(lambda e: e.tensor_single_scalar(out=rsrep[0:1, :],
                                             in_=rsrep[0:1, :], scalar=-0.5,
                                             op=AL.mult))
        k.wait("scalar", "s_ve")
        sop(lambda e: e.activation(out=rsrep[0:1, :], in_=rsrep[0:1, :],
                                   func=AF.Exp))
        bank_wait("b5")
        k.wait("tensor", "s_sc")
        top(lambda e: e.matmul(pb5[:], onesf_c[0:1, :], murep[0:1, :],
                               start=True, stop=True))
        k.wait("vector", "s_pe")
        vop(lambda e: e.tensor_copy(out=murep[:], in_=pb5[:]))
        bank_consumed("b5", "s_ve")
        bank_wait("b5")
        k.wait("tensor", "s_ve")
        top(lambda e: e.matmul(pb5[:], onesf_c[0:1, :], rsrep[0:1, :],
                               start=True, stop=True))
        k.wait("vector", "s_pe")
        vop(lambda e: e.tensor_copy(out=rsrep[:], in_=pb5[:]))
        bank_consumed("b5", "s_ve")

    def mm_chain(w_dram, nM, nK, rhs_of_kc, evac, kpart=128, kc_groups=None):
        for m in range(nM):
            groups = kc_groups if kc_groups is not None else [(0, nK)]
            loads = []
            for (k0, kn) in groups:
                if kc_groups is None:
                    srcap = w_dram[m]
                else:
                    srcap = w_dram[m][:, k0 * 128:(k0 + kn) * 128]
                slot, idx = ring_dma(srcap)
                loads.append((k0, kn, slot, idx))
            bank = pb[m % 2]
            bname = f"b{m % 2}"
            bank_wait(bname)
            for (k0, kn, slot, idx) in loads:
                ring_wait(idx)
                for j in range(kn):
                    kc = k0 + j
                    top(lambda e, slot=slot, j=j, kc=kc, bank=bank: e.matmul(
                        bank[:],
                        wring[0:kpart, slot * 1024 + j * 128: slot * 1024 + j * 128 + 128],
                        rhs_of_kc(kc),
                        start=(kc == 0), stop=(kc == nK - 1)))
                ring_done()
            evac(m, bank, bname)

    for t in range(NTILE):
        w0 = (t == 0)
        slot_cur = t % 2
        slot_prev = 1 - slot_cur

        if t > 0:
            k.wait("sync", "s_st", store_mark[t - 1])
        xsrc = dr["x_fm"].rearrange("(c p) tt -> p c tt", p=128)[:, :, t * TT:(t + 1) * TT]
        k.dma("sync", x_t[:].rearrange("p (c tt) -> p c tt", c=8), xsrc, "s_xt")
        x_ld = k.c["s_xt"]

        k.do("gpsimd", lambda e: e.iota(iot[:], pattern=[[1, TT]], base=t * TT,
                                        channel_multiplier=0,
                                        allow_small_or_imprecise_dtypes=True),
             inc=(("s_gpc", 1),))
        k.wait("vector", "s_gpc")
        vop(lambda e: e.tensor_scalar(out=rsrep[:], in0=iot[:], scalar1=-2048.0,
                                      scalar2=1.0 / 256.0, op0=AL.add, op1=AL.mult))
        vop(lambda e: e.tensor_single_scalar(out=rsrep[:], in_=rsrep[:],
                                             scalar=lnbase[:, 0:1], op=AL.mult))
        k.wait("scalar", "s_ve")
        sop(lambda e: e.activation(out=murep[:], in_=rsrep[:], func=AF.Exp,
                                   scale=-1.0))
        sop(lambda e: e.activation(out=rsrep[:], in_=rsrep[:], func=AF.Exp))
        k.wait("vector", "s_sc")
        vop(lambda e: e.tensor_single_scalar(out=tgen[:], in_=iot[:],
                                             scalar=invf[:, 0:1], op=AL.mult))

        def range_reduce(shift):
            iti = t1b[:].bitcast(I32)[:, 0:TT]
            itf = t1b[:].bitcast(F32)[:, TT:2 * TT]
            vop(lambda e, shift=shift: e.tensor_scalar(
                out=tgen2[:], in0=tgen[:], scalar1=shift,
                scalar2=1.0 / TWO_PI, op0=AL.add, op1=AL.mult))
            vop(lambda e, iti=iti: e.tensor_copy(out=iti, in_=tgen2[:]))
            vop(lambda e, iti=iti, itf=itf: e.tensor_copy(out=itf, in_=iti))
            vop(lambda e, shift=shift, itf=itf: e.tensor_single_scalar(
                out=tgen2[:], in_=tgen[:], scalar=shift, op=AL.add))
            vop(lambda e, itf=itf: e.scalar_tensor_tensor(
                out=tgen2[:], in0=itf, scalar=-TWO_PI, in1=tgen2[:],
                op0=AL.mult, op1=AL.add))
            vop(lambda e, itf=itf: e.tensor_single_scalar(
                out=itf, in_=tgen2[:], scalar=-PI, op=AL.is_lt))
            vop(lambda e, itf=itf: e.scalar_tensor_tensor(
                out=tgen2[:], in0=itf, scalar=TWO_PI, in1=tgen2[:],
                op0=AL.mult, op1=AL.add))
            vop(lambda e, itf=itf: e.tensor_single_scalar(
                out=itf, in_=tgen2[:], scalar=PI, op=AL.is_gt))
            vop(lambda e, itf=itf: e.scalar_tensor_tensor(
                out=tgen2[:], in0=itf, scalar=-TWO_PI, in1=tgen2[:],
                op0=AL.mult, op1=AL.add))

        range_reduce(0.0)
        k.wait("scalar", "s_ve")
        sop(lambda e: e.activation(out=tgen2[:], in_=tgen2[:], func=AF.Sin))
        k.wait("vector", "s_sc")
        vop(lambda e: e.tensor_tensor(out=rotT[:, 1 * TT:2 * TT], in0=tgen2[:],
                                      in1=rsrep[:], op=AL.mult))
        vop(lambda e: e.tensor_tensor(out=rotT[:, 3 * TT:4 * TT], in0=tgen2[:],
                                      in1=murep[:], op=AL.mult))
        range_reduce(PI / 2.0)
        k.wait("scalar", "s_ve")
        sop(lambda e: e.activation(out=tgen2[:], in_=tgen2[:], func=AF.Sin))
        k.wait("vector", "s_sc")
        vop(lambda e: e.tensor_tensor(out=rotT[:, 0 * TT:1 * TT], in0=tgen2[:],
                                      in1=rsrep[:], op=AL.mult))
        vop(lambda e: e.tensor_tensor(out=rotT[:, 2 * TT:3 * TT], in0=tgen2[:],
                                      in1=murep[:], op=AL.mult))

        k.wait("vector", "s_xt", x_ld)
        vop(lambda e: e.tensor_copy(out=scr8b[:], in_=x_t[:]))
        ln_stats(lambda c: scr8b[:, c * TT:(c + 1) * TT], xn, 1e-5)
        for c in range(8):
            vop(lambda e, c=c: e.tensor_tensor(
                out=yatt[:, c * TT:(c + 1) * TT],
                in0=x_t[:, c * TT:(c + 1) * TT], in1=murep[:], op=AL.subtract))
        for c in range(8):
            vop(lambda e, c=c: e.tensor_tensor(
                out=xffn[:, c * (TT + 1) + 1: c * (TT + 1) + 1 + TT],
                in0=yatt[:, c * TT:(c + 1) * TT], in1=rsrep[:], op=AL.mult))
        for c in range(8):
            vop(lambda e, c=c: e.tensor_copy(
                out=xn[:, c * TT:(c + 1) * TT],
                in_=xffn[:, c * (TT + 1) + 1: c * (TT + 1) + 1 + TT]))
        k.barrier()

        def at_evac(m, bank, bname):
            dst = [q_at, k_at, v_at][m // 8]
            pcol = m % 8
            k.wait("vector", "s_pe")
            vop(lambda e: e.tensor_single_scalar(
                out=dst[:, pcol * TT:(pcol + 1) * TT], in_=bank[:],
                scalar=bias_qkv[:, m:m + 1], op=AL.add))
            bank_consumed(bname, "s_ve")

        mm_chain(dr["aw_qkv"], 24, 8, lambda kc: xn[:, kc * TT:(kc + 1) * TT],
                 at_evac)
        qkv_sc = k.c["s_sc"]
        k.barrier()

        kf_mark = [None]
        for p in range(NP):
            bank_wait("b5")
            k.wait("tensor", "s_sc", qkv_sc)
            top(lambda e, p=p: e.matmul(pb5[:], wsta[:, p * 128:(p + 1) * 128],
                                        q_at[:, p * TT:(p + 1) * TT],
                                        start=True, stop=True))
            k.wait("scalar", "s_pe")
            sop(lambda e, p=p: e.activation(out=yatt[:, p * TT:(p + 1) * TT],
                                            in_=pb5[:], func=AF.Copy))
            bank_consumed("b5", "s_sc")
            for jg in range(NJG):
                j0 = jg * JG
                if kf_mark[0] is not None:
                    k.wait("sync", "s_pe", kf_mark[0])
                k.wait("sync", "s_kf")
                for h2 in range(2):
                    src = k_at[h2 * 64 + j0: h2 * 64 + j0 + JG,
                               p * TT:(p + 1) * TT]
                    dst = kqflat[h2 * 64: h2 * 64 + 1, :].rearrange(
                        "o (j tt) -> o j tt", j=JG)
                    k.dma("sync", dst, src, "s_kf")
                kq_ld = k.c["s_kf"]
                for jj in range(JG):
                    bank = pb[2 + jj % 2]
                    bname = f"b{2 + jj % 2}"
                    bank_wait(bname)
                    k.wait("tensor", "s_kf", kq_ld)
                    top(lambda e, jj=jj, bank=bank: e.matmul(
                        bank[:], ind_c[:], kqflat[:, jj * TT:(jj + 1) * TT],
                        start=True, stop=True))
                    k.wait("vector", "s_pe")
                    vop(lambda e, jj=jj, bank=bank, p=p: e.tensor_tensor(
                        out=t1b[:, jj * TT:(jj + 1) * TT], in0=bank[:],
                        in1=v_at[:, p * TT:(p + 1) * TT], op=AL.mult))
                    bank_consumed(bname, "s_ve")
                for jj in range(JG):
                    jcol = p * dh + j0 + jj
                    vop(lambda e, jj=jj, jcol=jcol: e.tensor_tensor_scan(
                        out=straj[:, jj * TT:(jj + 1) * TT],
                        data0=rt[:, jcol:jcol + 1].broadcast_to([128, TT]),
                        data1=t1b[:, jj * TT:(jj + 1) * TT],
                        initial=carry[:, jcol:jcol + 1],
                        op0=AL.mult, op1=AL.add))
                vop(lambda e, p=p, j0=j0: e.tensor_copy(
                    out=carry[:, p * dh + j0: p * dh + j0 + JG],
                    in_=straj[:].rearrange("q (j tt) -> q j tt", j=JG)[:, :, TT - 1]))
                k.wait("sync", "s_pe")
                k.wait("sync", "s_kf")
                for h2 in range(2):
                    src = q_at[h2 * 64 + j0: h2 * 64 + j0 + JG,
                               p * TT:(p + 1) * TT]
                    dst = kqflat[h2 * 64: h2 * 64 + 1, :].rearrange(
                        "o (j tt) -> o j tt", j=JG)
                    k.dma("sync", dst, src, "s_kf")
                kq_ld = k.c["s_kf"]
                for jj in range(JG):
                    bank = pb[2 + jj % 2]
                    bname = f"b{2 + jj % 2}"
                    bank_wait(bname)
                    k.wait("tensor", "s_kf", kq_ld)
                    top(lambda e, jj=jj, bank=bank: e.matmul(
                        bank[:], ind_c[:], kqflat[:, jj * TT:(jj + 1) * TT],
                        start=True, stop=True))
                    jcol = p * dh + j0 + jj
                    k.wait("vector", "s_pe")
                    vop(lambda e, jj=jj, bank=bank, jcol=jcol:
                        e.scalar_tensor_tensor(
                            out=fbuf[:, jj * TT:(jj + 1) * TT],
                            in0=straj[:, jj * TT:(jj + 1) * TT],
                            scalar=gt[:, jcol:jcol + 1], in1=bank[:],
                            op0=AL.mult, op1=AL.mult))
                    bank_consumed(bname, "s_ve")
                kf_mark[0] = k.c["s_pe"]
                f3 = fbuf[:].rearrange("q (j tt) -> q j tt", j=JG)
                for wdt in (4, 2, 1):
                    vop(lambda e, wdt=wdt, f3=f3: e.tensor_tensor(
                        out=f3[:, 0:wdt, :], in0=f3[:, 0:wdt, :],
                        in1=f3[:, wdt:2 * wdt, :], op=AL.add))
                k.wait("vector", "s_sc")
                vop(lambda e, p=p: e.tensor_tensor(
                    out=yatt[:, p * TT:(p + 1) * TT],
                    in0=yatt[:, p * TT:(p + 1) * TT],
                    in1=fbuf[:, 0:TT], op=AL.add))
        k.barrier()

        vop(lambda e: e.tensor_copy(out=t1b[:], in_=yatt[:]))
        yb_ve = k.c["s_ve"]

        def out_evac(m, bank, bname):
            k.wait("vector", "s_pe")
            vop(lambda e, m=m, bank=bank: e.scalar_tensor_tensor(
                out=x_t[:, m * TT:(m + 1) * TT], in0=bank[:],
                scalar=bias_out[:, m:m + 1], in1=x_t[:, m * TT:(m + 1) * TT],
                op0=AL.add, op1=AL.add))
            bank_consumed(bname, "s_ve")

        k.wait("tensor", "s_ve", yb_ve)
        mm_chain(dr["aw_out"], 8, 8, lambda kc: t1b[:, kc * TT:(kc + 1) * TT],
                 out_evac)
        k.barrier()

        for l in range(2):
            hs = 2 * l
            vop(lambda e, hs=hs: e.tensor_copy(
                out=xffn[:].rearrange("p (c tt) -> p c tt", c=8)[:, :, 0],
                in_=halo[:, hs * 8:(hs + 1) * 8]))
            vop(lambda e, hs=hs: e.tensor_copy(
                out=halo[:, hs * 8:(hs + 1) * 8],
                in_=xffn[:].rearrange("p (c tt) -> p c tt", c=8)[:, :, TT]))
            xv = xffn[:].rearrange("p (c tt) -> p c tt", c=8)
            vop(lambda e, xv=xv: e.tensor_copy(
                out=xn[:].rearrange("p (c tt) -> p c tt", c=8)[:, 0:4, :],
                in_=xv[:, 0:4, 1:TT + 1]))
            vop(lambda e, xv=xv: e.tensor_copy(
                out=xn[:].rearrange("p (c tt) -> p c tt", c=8)[:, 4:8, :],
                in_=xv[:, 4:8, 0:TT]))
            hb_ve = k.c["s_ve"]
            k.wait("tensor", "s_ve", hb_ve)

            def fqkv_evac(m, bank, bname, l=l):
                if m >= 8:
                    kcol = ((l * 2 + slot_cur) * 4 + (m - 8)) * TT
                    k.wait("scalar", "s_pe")
                    sop(lambda e, kcol=kcol, bank=bank: e.activation(
                        out=vhist[:, kcol:kcol + TT], in_=bank[:], func=AF.Copy))
                    bank_consumed(bname, "s_sc")
                else:
                    dst = straj if m < 4 else fbuf
                    pcol = (m % 4) * TT
                    k.wait("scalar", "s_pe")
                    sop(lambda e, dst=dst, pcol=pcol, bank=bank: e.activation(
                        out=dst[:, pcol:pcol + TT], in_=bank[:], func=AF.Copy))
                    bank_consumed(bname, "s_sc")

            mm_chain(dr["f_qkv"][l], 12, 8,
                     lambda kc: xn[:, kc * TT:(kc + 1) * TT], fqkv_evac)
            k.barrier()

            for part in range(2):
                raw = straj if part == 0 else fbuf
                sccol = qsc if part == 0 else ksc
                cosT = rotT[:, 0 * TT:1 * TT] if part == 0 else rotT[:, 2 * TT:3 * TT]
                sinT = rotT[:, 1 * TT:2 * TT] if part == 0 else rotT[:, 3 * TT:4 * TT]
                for s in range(4):
                    sl = raw[:, s * TT:(s + 1) * TT]
                    vop(lambda e, sl=sl: e.tensor_tensor(
                        out=scr8b[:, 0:TT], in0=sl, in1=sl, op=AL.mult))
                    bank_wait("b5")
                    k.wait("tensor", "s_ve")
                    top(lambda e: e.matmul(pb5[:], sumrep_c[:], scr8b[:, 0:TT],
                                           start=True, stop=True))
                    k.wait("vector", "s_pe")
                    vop(lambda e: e.tensor_single_scalar(out=tgen[:], in_=pb5[:],
                                                         scalar=1e-24, op=AL.add))
                    bank_consumed("b5", "s_ve")
                    k.wait("scalar", "s_ve")
                    sop(lambda e: e.activation(out=tgen[:], in_=tgen[:],
                                               func=AF.Ln))
                    k.wait("vector", "s_sc")
                    vop(lambda e: e.tensor_single_scalar(out=tgen[:], in_=tgen[:],
                                                         scalar=-0.5, op=AL.mult))
                    k.wait("scalar", "s_ve")
                    sop(lambda e: e.activation(out=tgen[:], in_=tgen[:],
                                               func=AF.Exp))
                    k.wait("vector", "s_sc")
                    vop(lambda e, sccol=sccol, l=l: e.tensor_scalar(
                        out=tgen[:], in0=tgen[:], scalar1=1e12,
                        scalar2=sccol[:, l:l + 1], op0=AL.min, op1=AL.mult))
                    vop(lambda e, sl=sl: e.tensor_tensor(
                        out=scr8b[:, TT:2 * TT], in0=sl, in1=tgen[:], op=AL.mult))
                    bank_wait("b5")
                    k.wait("tensor", "s_ve")
                    top(lambda e: e.matmul(pb5[:], perm_c[:], scr8b[:, TT:2 * TT],
                                           start=True, stop=True))
                    vop(lambda e, cosT=cosT: e.tensor_tensor(
                        out=scr8b[:, 2 * TT:3 * TT], in0=scr8b[:, TT:2 * TT],
                        in1=cosT, op=AL.mult))
                    k.wait("vector", "s_pe")
                    vop(lambda e, sinT=sinT: e.tensor_tensor(
                        out=scr8b[:, 3 * TT:4 * TT], in0=pb5[:], in1=sinT,
                        op=AL.mult))
                    bank_consumed("b5", "s_ve")
                    if part == 0:
                        dst = qhf[:, s * TT:(s + 1) * TT]
                    else:
                        kcol = ((l * 2 + slot_cur) * 4 + s) * TT
                        dst = khist[:, kcol:kcol + TT]
                    vop(lambda e, dst=dst: e.tensor_tensor(
                        out=dst, in0=scr8b[:, 2 * TT:3 * TT],
                        in1=scr8b[:, 3 * TT:4 * TT], op=AL.add))
            k.barrier()

            for h in range(FH):
                base = (h % 2) * 64
                slab = h // 2
                bank_wait("bt")
                k.wait("tensor", "s_ve")
                for kb in range(8):
                    sl = slot_prev if kb < 4 else slot_cur
                    kcol = ((l * 2 + sl) * 4 + slab) * TT + (kb % 4) * 128
                    top(lambda e, base=base, kcol=kcol, kb=kb: e.transpose(
                        pbt[:, kb * 64:(kb + 1) * 64],
                        vhist[base:base + 64, kcol:kcol + 128],
                        eye_c[base:base + 64, base:base + 64]))
                k.wait("scalar", "s_pe")
                sop(lambda e: e.activation(out=vtb[:, 0:512], in_=pbt[:, 0:512],
                                           func=AF.Copy))
                bank_consumed("bt", "s_sc")
                vop(lambda e: e.memset(sexpT[:], 0.0))
                sexpT_zero_ve = k.c["s_ve"]
                for qc in range(4):
                    k.wait("vector", "s_sc")
                    segs = []
                    lenA = 512 - 128 * qc
                    if lenA > 0:
                        segs.append((0, lenA, slot_prev, 128 * qc))
                    if qc > 0:
                        segs.append((lenA, 128 * qc, slot_cur, 0))
                    segs.append((512, 128, slot_cur, 128 * qc))
                    bank_wait("bs")
                    k.wait("tensor", "s_sc")
                    k.wait("tensor", "s_ve")
                    for (d0, ln, sl, s0) in segs:
                        kcol = ((l * 2 + sl) * 4 + slab) * TT + s0
                        top(lambda e, d0=d0, ln=ln, kcol=kcol, base=base, qc=qc:
                            e.matmul(pbs[:, d0:d0 + ln],
                                     qhf[base:base + 64,
                                         slab * TT + qc * 128: slab * TT + qc * 128 + 128],
                                     khist[base:base + 64, kcol:kcol + ln],
                                     start=True, stop=True))
                    k.wait("vector", "s_pe")
                    vop(lambda e: e.tensor_tensor(out=sraw[:, 0:640], in0=pbs[:],
                                                  in1=mask_c[:], op=AL.add))
                    bank_consumed("bs", "s_ve")
                    vstart = (512 - 128 * qc) if w0 else 0
                    vop(lambda e, vstart=vstart, qc=qc: e.reduce_max(
                        out=rowbuf[:, qc:qc + 1], in_=sraw[:, vstart:640],
                        axis=AX.X, negate=True))
                    k.wait("scalar", "s_ve")
                    k.wait("scalar", "s_pe")
                    sop(lambda e, vstart=vstart, qc=qc: e.activation(
                        out=sexp1[:, vstart:640], in_=sraw[:, vstart:640],
                        func=AF.Exp, bias=rowbuf[:, qc:qc + 1],
                        accum_out=rowbuf[:, 4 + qc:5 + qc]))
                    kb_lo = max(qc, 4) if w0 else qc
                    kb_hi = qc + 4
                    nb = kb_hi - kb_lo + 1
                    bank_wait("bt")
                    k.wait("tensor", "s_sc")
                    k.wait("tensor", "s_ve", sexpT_zero_ve)
                    for i, kb in enumerate(range(kb_lo, kb_hi + 1)):
                        joff = (kb - qc) * 128
                        top(lambda e, i=i, joff=joff: e.transpose(
                            pbt[:, i * 128:(i + 1) * 128],
                            sexp1[:, joff:joff + 128], eye_c[:]))
                    k.wait("scalar", "s_pe")
                    dstT = sexpT[:].rearrange("p (kb q) -> p kb q", kb=8)
                    sop(lambda e, dstT=dstT, kb_lo=kb_lo, nb=nb, qc=qc:
                        e.activation(
                            out=dstT[:, kb_lo:kb_lo + nb, qc * 128:(qc + 1) * 128],
                            in_=pbt[:, 0:nb * 128].rearrange(
                                "p (b q) -> p b q", b=nb),
                            func=AF.Copy))
                    bank_consumed("bt", "s_sc")
                k.wait("vector", "s_sc")
                vop(lambda e: e.reciprocal(out=rowbuf[:, 8:12], in_=rowbuf[:, 4:8]))
                bank_wait("b5")
                k.wait("tensor", "s_ve")
                for qc in range(4):
                    top(lambda e, qc=qc: e.transpose(
                        pb5[0:1, qc * 128:(qc + 1) * 128],
                        rowbuf[:, 8 + qc:9 + qc], g["eyef_c"][:]))
                k.wait("vector", "s_pe")
                vop(lambda e: e.tensor_copy(out=tgen[0:1, 0:512], in_=pb5[0:1, 0:512]))
                bank_consumed("b5", "s_ve")
                bank_wait("b5")
                k.wait("tensor", "s_ve")
                top(lambda e: e.matmul(pb5[:], onesf_c[0:1, :], tgen[0:1, 0:512],
                                       start=True, stop=True))
                k.wait("scalar", "s_pe")
                sop(lambda e: e.activation(out=rsrep[:], in_=pb5[:],
                                           func=AF.Copy))
                bank_consumed("b5", "s_sc")
                bank_wait("b1")
                k.wait("tensor", "s_sc")
                for kb in range(8):
                    top(lambda e, kb=kb: e.matmul(
                        pb[1][0:64, :], vtb[:, kb * 64:(kb + 1) * 64],
                        sexpT[:, kb * TT:(kb + 1) * TT],
                        start=(kb == 0), stop=(kb == 7)))
                k.wait("vector", "s_pe")
                k.wait("vector", "s_sc")
                vop(lambda e, h=h: e.tensor_tensor(
                    out=avo[0:64, h * TT:(h + 1) * TT], in0=pb[1][0:64, :],
                    in1=rsrep[0:64, :], op=AL.mult))
                bank_consumed("b1", "s_ve")
            k.barrier()

            def wo_evac(m, bank, bname, l=l):
                k.wait("vector", "s_pe")
                vop(lambda e, m=m, bank=bank: e.tensor_tensor(
                    out=xffn[:, m * (TT + 1) + 1: m * (TT + 1) + 1 + TT],
                    in0=xffn[:, m * (TT + 1) + 1: m * (TT + 1) + 1 + TT],
                    in1=bank[:], op=AL.add))
                bank_consumed(bname, "s_ve")

            mm_chain(dr["f_wo"][l], 8, 8,
                     lambda kc: avo[0:64, kc * TT:(kc + 1) * TT], wo_evac,
                     kpart=64)
            k.barrier()

            hs2 = 2 * l + 1
            vop(lambda e, hs2=hs2: e.tensor_copy(
                out=xffn[:].rearrange("p (c tt) -> p c tt", c=8)[:, :, 0],
                in_=halo[:, hs2 * 8:(hs2 + 1) * 8]))
            vop(lambda e, hs2=hs2: e.tensor_copy(
                out=halo[:, hs2 * 8:(hs2 + 1) * 8],
                in_=xffn[:].rearrange("p (c tt) -> p c tt", c=8)[:, :, TT]))
            xv2 = xffn[:].rearrange("p (c tt) -> p c tt", c=8)
            vop(lambda e, xv2=xv2: e.tensor_copy(
                out=scr8b[:].rearrange("p (c tt) -> p c tt", c=8)[:, 0:4, :],
                in_=xv2[:, 0:4, 1:TT + 1]))
            vop(lambda e, xv2=xv2: e.tensor_copy(
                out=scr8b[:].rearrange("p (c tt) -> p c tt", c=8)[:, 4:8, :],
                in_=xv2[:, 4:8, 0:TT]))
            ln_stats(lambda c: scr8b[:, c * TT:(c + 1) * TT], xn, 1e-5)
            for c in range(8):
                src = (xv2[:, c, 1:TT + 1] if c < 4 else xv2[:, c, 0:TT])
                vop(lambda e, c=c, src=src: e.tensor_tensor(
                    out=yatt[:, c * TT:(c + 1) * TT], in0=src, in1=murep[:],
                    op=AL.subtract))
            for c in range(8):
                vop(lambda e, c=c: e.tensor_tensor(
                    out=yatt[:, c * TT:(c + 1) * TT],
                    in0=yatt[:, c * TT:(c + 1) * TT], in1=rsrep[:], op=AL.mult))
            for c in range(8):
                vop(lambda e, c=c, l=l: e.tensor_scalar(
                    out=xn[:, c * TT:(c + 1) * TT],
                    in0=yatt[:, c * TT:(c + 1) * TT],
                    scalar1=lngt[:, l * 8 + c:l * 8 + c + 1],
                    scalar2=lnbt[:, l * 8 + c:l * 8 + c + 1],
                    op0=AL.mult, op1=AL.add))
            k.barrier()

            ab_state = {}

            def w1_evac(m, bank, bname, l=l):
                k.wait("scalar", "s_pe")
                if m % 2 == 0:
                    k.wait("vector", "s_pe")
                    k.wait("vector", "s_sc")
                    vop(lambda e, m=m, bank=bank, l=l: e.tensor_single_scalar(
                        out=t1b[:, 0:TT], in_=bank[:],
                        scalar=bias_b1[:, l * 44 + m:l * 44 + m + 1], op=AL.add))
                    bank_consumed(bname, "s_ve")
                else:
                    k.wait("scalar", "s_ve")
                    sop(lambda e, m=m, bank=bank, l=l: e.activation(
                        out=t1b[:, TT:2 * TT], in_=bank[:], func=AF.Gelu,
                        bias=bias_b1[:, l * 44 + m:l * 44 + m + 1]))
                    bank_consumed(bname, "s_sc")
                    c = m // 2
                    k.wait("vector", "s_sc")
                    vop(lambda e, c=c: e.tensor_tensor(
                        out=htb[:, c * TT:(c + 1) * TT], in0=t1b[:, 0:TT],
                        in1=t1b[:, TT:2 * TT], op=AL.mult))

            mm_chain(dr["f_w1"][l], 44, 8,
                     lambda kc: xn[:, kc * TT:(kc + 1) * TT], w1_evac)
            k.barrier()

            def w2_evac(m, bank, bname, l=l):
                k.wait("vector", "s_pe")
                vop(lambda e, m=m, bank=bank, l=l: e.scalar_tensor_tensor(
                    out=xffn[:, m * (TT + 1) + 1: m * (TT + 1) + 1 + TT],
                    in0=bank[:], scalar=bias_b2[:, l * 8 + m:l * 8 + m + 1],
                    in1=xffn[:, m * (TT + 1) + 1: m * (TT + 1) + 1 + TT],
                    op0=AL.add, op1=AL.add))
                bank_consumed(bname, "s_ve")

            mm_chain(dr["f_w2"][l], 8, NAB,
                     lambda kc: htb[:, kc * TT:(kc + 1) * TT], w2_evac,
                     kc_groups=[(0, 8), (8, 8), (16, 6)])
            k.barrier()

        for c in range(8):
            vop(lambda e, c=c: e.tensor_tensor(
                out=x_t[:, c * TT:(c + 1) * TT],
                in0=x_t[:, c * TT:(c + 1) * TT],
                in1=xffn[:, c * (TT + 1) + 1: c * (TT + 1) + 1 + TT],
                op=AL.add))
        fin_ve = k.c["s_ve"]
        k.wait("sync", "s_ve", fin_ve)
        ydst = dr["y_fm_out"].rearrange("(c p) tt -> p c tt", p=128)[
            :, :, t * TT:(t + 1) * TT]
        k.dma("sync", ydst, x_t[:].rearrange("p (c tt) -> p c tt", c=8), "s_st")
        store_mark.append(k.c["s_st"])
        k.barrier()

    k.wait("sync", "s_st", 16 * NTILE)




def build_nc():
    nc = bass.Bass("TRN2", target_bir_lowering=False, debug=False)
    dr = {}

    def din(name, shape, dt=F32):
        dr[name] = nc.dram_tensor(name, list(shape), dt, kind="ExternalInput").ap()

    din("x_fm", [D, T])
    din("aw_qkv", [24, 128, 1024])
    din("aw_qkv_b", [128, 24])
    din("aw_out", [8, 128, 1024])
    din("aw_out_b", [128, 8])
    din("a_wsta", [128, NP * 128])
    din("a_lam", [128, NP * dh])
    din("a_gam", [128, NP * dh])
    din("f_qkv", [2, 12, 128, 1024])
    din("f_wo", [2, 8, 64, 1024])
    din("f_w1", [2, 44, 128, 1024])
    din("f_b1", [128, 2 * 44])
    din("f_w2", [2, 8, 128, NAB * 128])
    din("f_b2", [128, 2 * 8])
    din("f_lng", [128, 2 * 8])
    din("f_lnb", [128, 2 * 8])
    din("f_qs", [128, 2])
    din("f_ks", [128, 2])
    din("c_ones1", [128, 1])
    din("c_onesf", [1, 128])
    din("c_ind", [128, 128])
    din("c_sumrep", [128, 128])
    din("c_perm", [128, 128])
    din("c_eye", [128, 128])
    din("c_mask", [128, 640])
    din("c_col", [128, 4])
    dr["y_fm_out"] = nc.dram_tensor("y_fm", [D, T], F32,
                                    kind="ExternalOutput").ap()

    es = ExitStack()

    def S(name, shape, dt):
        return es.enter_context(nc.sbuf_tensor(name, list(shape), dt))

    def PS(name, shape, dt):
        return es.enter_context(nc.psum_tensor(name, list(shape), dt))

    g = {}
    g["x_t"] = S("x_t", [128, 8 * TT], F32)
    g["xn"] = S("xn", [128, 8 * TT], BF16)
    g["xffn"] = S("xffn", [128, 8 * (TT + 1)], F32)
    g["q_at"] = S("q_at", [128, NP * TT], BF16)
    g["k_at"] = S("k_at", [128, NP * TT], BF16)
    g["v_at"] = S("v_at", [128, NP * TT], BF16)
    g["t1b"] = S("t1b", [128, JG * TT], BF16)
    g["straj"] = S("straj", [128, JG * TT], BF16)
    g["fbuf"] = S("fbuf", [128, JG * TT], BF16)
    g["scr8b"] = S("scr8b", [128, 8 * TT], BF16)
    g["yatt"] = S("yatt", [128, NP * TT], F32)
    g["kqflat"] = g["scr8b"]
    g["rt"] = S("rt", [128, NP * dh], F32)
    g["gt"] = S("gt", [128, NP * dh], F32)
    g["carry"] = S("carry", [128, NP * dh], BF16)
    g["wsta"] = S("wsta", [128, NP * 128], BF16)
    g["ind_c"] = S("ind_c", [128, 128], BF16)
    g["sumrep_c"] = S("sumrep_c", [128, 128], BF16)
    g["perm_c"] = S("perm_c", [128, 128], BF16)
    g["ones_c"] = S("ones_c", [128, 1], BF16)
    g["eye_c"] = S("eye_c", [128, 128], BF16)
    g["eyef_c"] = S("eyef_c", [128, 128], F32)
    g["onesf_c"] = S("onesf_c", [1, 128], F32)
    g["mask_c"] = S("mask_c", [128, 640], F32)
    g["col_c"] = S("col_c", [128, 4], F32)
    g["khist"] = S("khist", [128, 16 * TT], BF16)
    g["vhist"] = S("vhist", [128, 16 * TT], BF16)
    g["htb"] = S("htb", [128, NAB * TT], BF16)
    g["sexpT"] = g["v_at"]
    g["qhf"] = S("qhf", [128, 4 * TT], BF16)
    g["rotT"] = S("rotT", [128, 4 * TT], BF16)
    g["avo"] = g["q_at"]
    g["sraw"] = g["yatt"]
    g["sexp1"] = S("sexp1", [128, 640], BF16)
    g["vtb"] = g["k_at"]
    g["murep"] = S("murep", [128, TT], F32)
    g["rsrep"] = S("rsrep", [128, TT], F32)
    g["rowbuf"] = S("rowbuf", [128, 16], F32)
    g["bias_qkv"] = S("bias_qkv", [128, 24], F32)
    g["bias_out"] = S("bias_out", [128, 8], F32)
    g["bias_b1"] = S("bias_b1", [128, 2 * 44], F32)
    g["bias_b2"] = S("bias_b2", [128, 2 * 8], F32)
    g["lngt"] = S("lngt", [128, 2 * 8], F32)
    g["lnbt"] = S("lnbt", [128, 2 * 8], F32)
    g["qsc"] = S("qsc", [128, 2], F32)
    g["ksc"] = S("ksc", [128, 2], F32)
    g["iot"] = S("iot", [128, TT], F32)
    g["tgen"] = S("tgen", [128, TT], F32)
    g["invf"] = S("invf", [128, 1], F32)
    g["lnbase"] = S("lnbase", [128, 1], F32)
    g["halo"] = S("halo", [128, 32], F32)
    g["wring"] = S("wring", [128, NRING * 1024], BF16)
    g["pb"] = [PS(f"pb{i}", [128, 512], F32) for i in range(4)]
    g["pbt"] = PS("pbt", [128, 1024], BF16)
    g["pb5"] = PS("pb5", [128, 512], F32)
    g["pbs"] = PS("pbs", [128, 640], F32)

    sems = {}
    for s in SEMS:
        sems[s] = es.enter_context(nc.semaphore(s))

    block = es.enter_context(nc.Block())

    def mk(eng):
        def body(_eng_obj):
            k = Walk(nc, eng, sems)
            emit(k, dr, g)
        return body

    for eng in ENG_LIST:
        getattr(block, eng)(mk(eng))

    es.close()
    return nc


def _arr_lhsT(w):
    K, M = w.shape
    nK, nM = K // 128, M // 128
    a = w.reshape(nK, 128, nM, 128).transpose(2, 1, 0, 3)
    return np.ascontiguousarray(a.reshape(nM, 128, nK * 128).astype(np.float32))


def _arr_lhsT64(w):
    K, M = w.shape
    nK, nM = K // 64, M // 128
    a = w.reshape(nK, 64, nM, 128).transpose(2, 1, 0, 3)
    return np.ascontiguousarray(a.reshape(nM, 64, nK * 128).astype(np.float32))


def _cols(vec, n):
    return np.ascontiguousarray(vec.reshape(n, 128).T.astype(np.float32))


def prepare(ii):
    x = ii["L0_attn_x"].astype(np.float32)

    wqkv = ii["L0_attn_Wqkv_w"].astype(np.float32)
    bqkv = ii["L0_attn_Wqkv_b"].astype(np.float32)
    outw = ii["L0_attn_out_w"].astype(np.float32)
    outb = ii["L0_attn_out_b"].astype(np.float32)
    wsta = ii["L0_attn_W_static"].astype(np.float32)
    lam = ii["L0_attn_Lambda_raw"].astype(np.float32)
    gam = ii["L0_attn_Gamma"].astype(np.float32)
    fqkv = ii["L0_ffn_wqkv"].astype(np.float32)
    fqs = ii["L0_ffn_q_scale"].astype(np.float32)
    fks = ii["L0_ffn_k_scale"].astype(np.float32)
    fwo = ii["L0_ffn_wo"].astype(np.float32)
    flng = ii["L0_ffn_ln_g"].astype(np.float32)
    flnb = ii["L0_ffn_ln_b"].astype(np.float32)
    fw1 = ii["L0_ffn_w1"].astype(np.float32)
    fb1 = ii["L0_ffn_b1"].astype(np.float32)
    fw2 = ii["L0_ffn_w2"].astype(np.float32)
    fb2 = ii["L0_ffn_b2"].astype(np.float32)

    aw_qkv = _arr_lhsT(wqkv.T)
    aw_qkv_b = _cols(bqkv, 24)
    aw_out = _arr_lhsT(outw.T)
    aw_out_b = _cols(outb, 8)
    a_wsta = np.zeros((128, NP * 128), np.float32)
    a_lam = np.zeros((128, NP * dh), np.float32)
    a_gam = np.zeros((128, NP * dh), np.float32)
    for p in range(NP):
        for h2 in range(2):
            h = 2 * p + h2
            a_wsta[h2 * 64:(h2 + 1) * 64,
                   p * 128 + h2 * 64: p * 128 + (h2 + 1) * 64] = wsta[h].T
            a_lam[h2 * 64:(h2 + 1) * 64, p * dh:(p + 1) * dh] = lam[h]
            a_gam[h2 * 64:(h2 + 1) * 64, p * dh:(p + 1) * dh] = gam[h]

    f_qkv = np.stack([_arr_lhsT(fqkv[l]) for l in range(2)])
    f_wo = np.stack([_arr_lhsT64(fwo[l]) for l in range(2)])
    HID = 2730
    w1p = np.zeros((2, D, HID2), np.float32)
    b1p = np.zeros((2, HID2), np.float32)
    for l in range(2):
        a_part = fw1[l][:, :HID]
        g_part = fw1[l][:, HID:]
        ba = fb1[l][:HID]
        bg = fb1[l][HID:]
        for c in range(NAB):
            lo, hi = c * 128, min((c + 1) * 128, HID)
            n = hi - lo
            if n > 0:
                w1p[l][:, (2 * c) * 128:(2 * c) * 128 + n] = a_part[:, lo:hi]
                w1p[l][:, (2 * c + 1) * 128:(2 * c + 1) * 128 + n] = g_part[:, lo:hi]
                b1p[l][(2 * c) * 128:(2 * c) * 128 + n] = ba[lo:hi]
                b1p[l][(2 * c + 1) * 128:(2 * c + 1) * 128 + n] = bg[lo:hi]
    f_w1 = np.stack([_arr_lhsT(w1p[l]) for l in range(2)])
    f_b1 = np.concatenate([_cols(b1p[l], 44) for l in range(2)], axis=1)
    w2p = np.zeros((2, HTK, D), np.float32)
    w2p[:, :HID, :] = fw2
    f_w2 = np.stack([_arr_lhsT(w2p[l]) for l in range(2)])
    f_b2 = np.concatenate([_cols(fb2[l], 8) for l in range(2)], axis=1)
    f_lng = np.concatenate([_cols(flng[l], 8) for l in range(2)], axis=1)
    f_lnb = np.concatenate([_cols(flnb[l], 8) for l in range(2)], axis=1)
    f_qs = np.ascontiguousarray(
        np.stack([np.tile(fqs[l], 2) for l in range(2)], axis=1))
    f_ks = np.ascontiguousarray(
        np.stack([np.tile(fks[l], 2) for l in range(2)], axis=1))

    c_ones1 = np.ones((128, 1), np.float32)
    c_onesf = np.ones((1, 128), np.float32)
    c_ind = np.zeros((128, 128), np.float32)
    c_ind[0, 0:64] = 1.0
    c_ind[64, 64:128] = 1.0
    c_sumrep = np.zeros((128, 128), np.float32)
    c_sumrep[0:64, 0:64] = 1.0
    c_sumrep[64:128, 64:128] = 1.0
    c_perm = np.zeros((128, 128), np.float32)
    for h2 in range(2):
        o = h2 * 64
        for i2 in range(32):
            c_perm[o + i2 + 32, o + i2] = -1.0
            c_perm[o + i2, o + i2 + 32] = 1.0
    c_eye = np.eye(128, dtype=np.float32)
    c_mask = np.zeros((128, 640), np.float32)
    for p in range(128):
        for j in range(640):
            if not (0 <= j - p <= 512):
                c_mask[p, j] = -1e30
    c_col = np.zeros((128, 4), np.float32)
    c_col[:, 0] = np.arange(128) % 32
    c_col[:, 1] = np.arange(128) % 64
    c_col[:, 2] = np.arange(128)
    c_col[:, 3] = 1.0

    shared = dict(
        aw_qkv=aw_qkv, aw_qkv_b=aw_qkv_b, aw_out=aw_out, aw_out_b=aw_out_b,
        a_wsta=a_wsta, a_lam=a_lam, a_gam=a_gam,
        f_qkv=f_qkv, f_wo=f_wo, f_w1=f_w1, f_b1=f_b1, f_w2=f_w2, f_b2=f_b2,
        f_lng=f_lng, f_lnb=f_lnb, f_qs=f_qs, f_ks=f_ks,
        c_ones1=c_ones1, c_onesf=c_onesf, c_ind=c_ind, c_sumrep=c_sumrep,
        c_perm=c_perm, c_eye=c_eye, c_mask=c_mask, c_col=c_col,
    )
    in_maps = []
    for b in range(B):
        m = dict(shared)
        m["x_fm"] = np.ascontiguousarray(x[b].T)
        in_maps.append(m)
    return in_maps


def kernel(**inputs):
    ii = {kk: np.asarray(v) for kk, v in inputs.items()}
    in_maps = prepare(ii)
    nc = build_nc()
    res = run_bass_kernel_spmd(nc, in_maps, core_ids=list(range(B)))
    out = np.empty((B, T, D), np.float32)
    for b in range(B):
        out[b] = res.results[b]["y_fm"].T
    return out
